# version 01 (final)
# baseline (speedup 1.0000x reference)
import numpy as np
import ml_dtypes
from contextlib import ExitStack

import concourse.bass as bass
import concourse.mybir as mybir
from concourse.bass_utils import run_bass_kernel_spmd
from concourse.alu_op_type import AluOpType as Al

F32 = mybir.dt.float32
BF16 = mybir.dt.bfloat16
AF = mybir.ActivationFunctionType
AX = mybir.AxisListType

S_LEN = 2048
E = 2048
I_DIM = 8192
EC = E // 128
ICN = I_DIM // 128
NB = 2
T_CORE = NB * S_LEN
C_SM = float(E) ** -0.5
LN_EPS = 1e-5
FW2048_RES = 1.0 / (4.0 * np.sqrt(2.0))
FW8192_RES = 1.0 / (8.0 * np.sqrt(2.0))
NEG_INF = -1.0e30

ENGS = ("sync", "tensor", "vector", "scalar", "gpsimd")
P1_TRUNC = None
DELAY_US = 0
SCRUB = False


class Prog:

    def __init__(self, nc):
        self.nc = nc
        self.ops = {e: [] for e in ENGS}
        self.cnt = {}
        self.semh = {}

    def sem(self, name, es):
        self.semh[name] = es.enter_context(self.nc.semaphore(name))
        self.cnt[name] = 0

    def op(self, eng, fn, inc=None, n=1):
        if inc is not None:
            self.cnt[inc] += n
            sem = self.semh[inc]
            self.ops[eng].append(lambda EE, fn=fn, sem=sem, n=n: fn(EE).then_inc(sem, n))
            return self.cnt[inc]
        self.ops[eng].append(fn)
        return None

    def dma(self, eng, out, in_, inc, **kw):
        return self.op(eng, lambda EE: EE.dma_start(out=out, in_=in_, **kw), inc=inc, n=16)

    def dmat(self, eng, out, in_, inc):
        return self.op(eng, lambda EE: EE.dma_start_transpose(out, in_), inc=inc, n=16)

    def wait(self, eng, sem, v):
        if not v:
            return
        semh = self.semh[sem]
        self.ops[eng].append(lambda EE: EE.wait_ge(semh, v))

    def v(self, sem):
        return self.cnt[sem]

    def snap(self):
        return dict(self.cnt)

    def wait_all(self, eng, snapshot):
        for s, v in snapshot.items():
            self.wait(eng, s, v)

    def replay(self, block):
        for name in ENGS:
            ops = self.ops[name]
            if not ops:
                continue

            def body(engobj, ops=ops):
                for f in ops:
                    f(engobj)

            getattr(block, name)(body)


def build_program(debug_phase=99):
    nc = bass.Bass("TRN2", target_bir_lowering=False, detect_race_conditions=False)

    x_in = nc.dram_tensor("x_in", [T_CORE, E], F32, kind="ExternalInput").ap()
    wq_sw = nc.dram_tensor("wq_sw", [EC, 128, E], F32, kind="ExternalInput").ap()
    wk_sw = nc.dram_tensor("wk_sw", [EC, 128, E], F32, kind="ExternalInput").ap()
    wv_sw = nc.dram_tensor("wv_sw", [EC, 128, E], F32, kind="ExternalInput").ap()
    wup_sw = nc.dram_tensor("wup_sw", [ICN, 128, E], F32, kind="ExternalInput").ap()
    wgate_sw = nc.dram_tensor("wgate_sw", [ICN, 128, E], F32, kind="ExternalInput").ap()
    wdown_t = nc.dram_tensor("wdown_t", [I_DIM, E], F32, kind="ExternalInput").ap()
    su_ug_b = nc.dram_tensor("su_ug_b", [128, E], F32, kind="ExternalInput").ap()
    su_dp_pc = nc.dram_tensor("su_dp_pc", [128, ICN], F32, kind="ExternalInput").ap()
    wsc_up_pc = nc.dram_tensor("wsc_up_pc", [128, ICN], F32, kind="ExternalInput").ap()
    wsc_gate_pc = nc.dram_tensor("wsc_gate_pc", [128, ICN], F32, kind="ExternalInput").ap()
    wsc_dp_b = nc.dram_tensor("wsc_dp_b", [128, E], F32, kind="ExternalInput").ap()
    ident32 = nc.dram_tensor("ident32", [128, 128], F32, kind="ExternalInput").ap()
    identb = nc.dram_tensor("identb", [128, 128], BF16, kind="ExternalInput").ap()
    h128b = nc.dram_tensor("h128b", [128, 128], BF16, kind="ExternalInput").ap()
    maskd = nc.dram_tensor("maskd", [128, 128], F32, kind="ExternalInput").ap()
    cconst = nc.dram_tensor("cconst", [128, 4], F32, kind="ExternalInput").ap()

    out = nc.dram_tensor("out", [T_CORE, E], F32, kind="ExternalOutput").ap()

    wd_s = nc.dram_tensor("wd_s", [4, ICN, 128, 512], BF16, kind="Internal").ap()
    v_s = nc.dram_tensor("v_s", [E, S_LEN], BF16, kind="Internal").ap()
    k_s = nc.dram_tensor("k_s", [EC, 4, 128, 512], BF16, kind="Internal").ap()
    q_s = nc.dram_tensor("q_s", [16, EC, 128, 128], BF16, kind="Internal").ap()
    x1_s = nc.dram_tensor("x1_s", [T_CORE, E], F32, kind="Internal").ap()
    z_s = nc.dram_tensor("z_s", [ICN, 128, T_CORE], BF16, kind="Internal").ap()

    dbg = {}
    if debug_phase < 99:
        dbg["xnT"] = nc.dram_tensor("dbg_xnT", [128, EC * S_LEN], BF16, kind="ExternalOutput").ap()
        dbg["kT"] = nc.dram_tensor("dbg_kT", [128, EC * S_LEN], BF16, kind="ExternalOutput").ap()
        dbg["v"] = nc.dram_tensor("dbg_v", [128, EC * S_LEN], BF16, kind="ExternalOutput").ap()
        dbg["x1"] = nc.dram_tensor("dbg_x1", [S_LEN, E], F32, kind="ExternalOutput").ap()
        dbg["xn2T"] = nc.dram_tensor("dbg_xn2T", [128, EC * T_CORE], BF16, kind="ExternalOutput").ap()
        dbg["wde"] = nc.dram_tensor("dbg_wde", [128, ICN * 512], BF16, kind="ExternalOutput").ap()
        dbg["z"] = nc.dram_tensor("dbg_z", [128, T_CORE], BF16, kind="ExternalOutput").ap()

    es = ExitStack()
    with es:
        P = Prog(nc)
        for s in ("qs", "sx0", "sx1", "sq0", "sq1", "sst0", "sst1", "sst2",
                  "sst3", "srl", "so0", "so1", "sxs", "qp", "qp2",
                  "pe", "act", "dve", "pool"):
            P.sem(s, es)
        SX = ("sx0", "sx1")
        SQ = ("sq0", "sq1")
        SST = ("sst0", "sst1", "sst2", "sst3")
        SO = ("so0", "so1")
        QP = ("qp", "qp2")

        i32sb = es.enter_context(nc.sbuf_tensor("i32sb", [128, 128], F32))
        ibsb = es.enter_context(nc.sbuf_tensor("ibsb", [128, 128], BF16))
        h128sb = es.enter_context(nc.sbuf_tensor("h128sb", [128, 128], BF16))
        masksb = es.enter_context(nc.sbuf_tensor("masksb", [128, 128], F32))
        ccsb = es.enter_context(nc.sbuf_tensor("ccsb", [128, 4], F32))
        sudpsb = es.enter_context(nc.sbuf_tensor("sudpsb", [128, ICN], F32))
        wupsc = es.enter_context(nc.sbuf_tensor("wupsc", [128, ICN], F32))
        wgatesc = es.enter_context(nc.sbuf_tensor("wgatesc", [128, ICN], F32))
        suugb = es.enter_context(nc.sbuf_tensor("suugb", [128, E], F32))
        wdpsb = es.enter_context(nc.sbuf_tensor("wdpsb", [128, E], F32))
        sumv = es.enter_context(nc.sbuf_tensor("sumv", [128, 2], F32))
        meanv = es.enter_context(nc.sbuf_tensor("meanv", [128, 1], F32))
        msqv = es.enter_context(nc.sbuf_tensor("msqv", [128, 1], F32))
        varv = es.enter_context(nc.sbuf_tensor("varv", [128, 1], F32))
        rstdv = es.enter_context(nc.sbuf_tensor("rstdv", [128, 1], F32))
        lnjunk = es.enter_context(nc.sbuf_tensor("lnjunk", [128, 2048], BF16))
        nmx = es.enter_context(nc.sbuf_tensor("nmx", [128, 1], F32))
        nmxs = es.enter_context(nc.sbuf_tensor("nmxs", [128, 1], F32))
        sums = es.enter_context(nc.sbuf_tensor("sums", [128, 1], F32))
        rsum = es.enter_context(nc.sbuf_tensor("rsum", [128, 1], F32))

        EPS = ccsb[:, 0:1]
        LNC2 = ccsb[:, 1:2]
        ZERO = ccsb[:, 2:3]
        INV2048 = ccsb[:, 3:4]

        for dst, src in [(i32sb, ident32), (ibsb, identb), (h128sb, h128b),
                         (masksb, maskd), (ccsb, cconst), (sudpsb, su_dp_pc),
                         (wupsc, wsc_up_pc), (wgatesc, wsc_gate_pc),
                         (suugb, su_ug_b), (wdpsb, wsc_dp_b)]:
            v_consts = P.dma("sync", dst[:], src, "qs")
        P.wait("sync", "qs", v_consts)
        for eng in ("tensor", "vector", "scalar", "gpsimd"):
            P.wait(eng, "qs", v_consts)

        def ln_stats(tile_ap, bias_c):
            P.wait("scalar", "dve", P.v("dve"))
            P.op("scalar", lambda EE: EE.activation(
                lnjunk[:], tile_ap, AF.Copy, bias=0.0, scale=1.0,
                accum_out=sumv[:, 0:1]))
            v_xr = P.op("scalar", lambda EE: EE.activation(
                lnjunk[:], tile_ap, AF.Square, bias=0.0, scale=1.0,
                accum_out=sumv[:, 1:2]), inc="act")
            P.wait("vector", "act", v_xr)
            P.op("vector", lambda EE: EE.tensor_scalar(
                meanv[:], sumv[:, 0:1], INV2048, None, Al.mult))
            vmq = P.op("vector", lambda EE: EE.tensor_scalar(
                msqv[:], meanv[:], meanv[:], None, Al.mult), inc="dve")
            P.wait("vector", "dve", vmq)
            v_sd = P.op("vector", lambda EE: EE.tensor_scalar(
                varv[:], sumv[:, 1:2], INV2048, msqv[:], Al.mult,
                Al.subtract), inc="dve")
            P.wait("scalar", "dve", v_sd)
            P.op("scalar", lambda EE: EE.activation(
                varv[:], varv[:], AF.Ln, bias=EPS, scale=1.0))
            v_rs = P.op("scalar", lambda EE, bias_c=bias_c: EE.activation(
                rstdv[:], varv[:], AF.Exp, bias=bias_c, scale=-0.5), inc="act")
            return v_xr, v_sd, v_rs

        def phase_barrier():
            s = P.snap()
            for eng in ENGS:
                P.wait_all(eng, s)

        if DELAY_US > 0:
            P.op("vector", lambda EE: EE.memset(lnjunk[:], 1.0))
            niter = max(1, int(DELAY_US / 2.3))
            for _ in range(niter):
                P.op("vector", lambda EE: EE.tensor_scalar(
                    lnjunk[:], lnjunk[:], 1.0, None, Al.mult))
        P.op("vector", lambda EE: EE.memset(lnjunk[:, 0:128], 1.0), inc="dve")
        P.wait("scalar", "dve", P.v("dve"))
        for fn in (AF.Copy, AF.Square, AF.Silu, AF.Ln, AF.Exp, AF.Ln, AF.Exp):
            P.op("scalar", lambda EE, fn=fn: EE.activation(
                lnjunk[:, 128:256], lnjunk[:, 0:128], fn, bias=0.0, scale=1.0,
                accum_out=sumv[:, 0:1]), inc="act")
        for sem in ("sx0", "sx1", "sq0", "sq1", "sst0", "sst1", "sst2", "sst3",
                    "srl"):
            P.dma("sync", sumv[:], cconst[:, 0:2], sem)
        for sem in ("so0", "so1"):
            P.op("scalar", lambda EE: EE.dma_start(
                out=sumv[:], in_=cconst[:, 0:2]), inc=sem, n=16)
        for sem in ("qp", "qp2", "sxs"):
            P.dma("gpsimd", lnjunk[:, 8:10], cconst[:, 0:2], sem)
        phase_barrier()

        with ExitStack() as p1:
            W1 = p1.enter_context(nc.sbuf_tensor("W1", [128, ICN * 512], BF16))
            W2 = p1.enter_context(nc.sbuf_tensor("W2", [128, ICN * 512], BF16))
            psA = p1.enter_context(nc.psum_tensor("psA", [128, 2048], F32))

            wdv = wdown_t.rearrange("(gg c p) d -> gg p c d", p=128, c=8)
            w1_free_dve = 0
            stash_done = 0
            grp_marks = {}
            gidx = 0
            for dsl in range(4):
                P.wait("gpsimd", "dve", w1_free_dve)
                for gg in range(8):
                    sgi = QP[gidx % 2]
                    if gidx >= 2:
                        P.wait("gpsimd", sgi, grp_marks[gidx - 2])
                    dst = W1[:, gg * 4096:(gg + 1) * 4096].rearrange(
                        "p (c d) -> p c d", d=512)
                    grp_marks[gidx] = P.dma(
                        "gpsimd", dst, wdv[gg, :, :, dsl * 512:(dsl + 1) * 512],
                        sgi)
                    gidx += 1
                ev_marks = []
                for i in range(ICN):
                    gi = dsl * 8 + i // 8
                    P.wait("tensor", QP[gi % 2], grp_marks[gi])
                    if i >= 4:
                        P.wait("tensor", "dve", ev_marks[i - 4])
                    slot = i % 4
                    vm = P.op("tensor", lambda EE, i=i, slot=slot: EE.matmul(
                        psA[:, slot * 512:(slot + 1) * 512], h128sb[:],
                        W1[:, i * 512:(i + 1) * 512], start=True, stop=True),
                        inc="pe")
                    P.wait("vector", "pe", vm)
                    if i == 0:
                        P.wait("vector", "sst0", stash_done)
                    ve = P.op("vector", lambda EE, i=i, slot=slot, dsl=dsl: EE.tensor_tensor(
                        W2[:, i * 512:(i + 1) * 512],
                        psA[:, slot * 512:(slot + 1) * 512],
                        wdpsb[:, dsl * 512:(dsl + 1) * 512], Al.mult),
                        inc="dve")
                    ev_marks.append(ve)
                bufs = [W2, W1]
                v_dve_stage = ev_marks[-1]
                v_pool_stage = P.v("pool")
                for si, h in enumerate([1, 2, 4, 8, 16, 32]):
                    src = bufs[si % 2]
                    dst = bufs[(si + 1) % 2]
                    sv = src.rearrange("p (blk two hf) -> p blk two hf",
                                       two=2, hf=h * 512)
                    dv = dst.rearrange("p (blk two hf) -> p blk two hf",
                                       two=2, hf=h * 512)
                    half = h * 512 // 2
                    P.wait("vector", "pool", v_pool_stage)
                    P.op("vector", lambda EE, sv=sv, dv=dv, half=half: EE.tensor_tensor(
                        dv[:, :, 0, 0:half], sv[:, :, 0, 0:half],
                        sv[:, :, 1, 0:half], Al.add))
                    vD = P.op("vector", lambda EE, sv=sv, dv=dv, half=half: EE.tensor_tensor(
                        dv[:, :, 1, 0:half], sv[:, :, 0, 0:half],
                        sv[:, :, 1, 0:half], Al.subtract), inc="dve")
                    P.wait("gpsimd", "dve", v_dve_stage)
                    P.op("gpsimd", lambda EE, sv=sv, dv=dv, half=half, h=h: EE.tensor_tensor(
                        dv[:, :, 0, half:h * 512], sv[:, :, 0, half:h * 512],
                        sv[:, :, 1, half:h * 512], Al.add))
                    vP = P.op("gpsimd", lambda EE, sv=sv, dv=dv, half=half, h=h: EE.tensor_tensor(
                        dv[:, :, 1, half:h * 512], sv[:, :, 0, half:h * 512],
                        sv[:, :, 1, half:h * 512], Al.subtract), inc="pool")
                    v_dve_stage = vD
                    v_pool_stage = vP
                P.wait("vector", "pool", v_pool_stage)
                for gg in range(8):
                    vs = None
                    for cc in range(8):
                        i = gg * 8 + cc
                        vs = P.op("vector", lambda EE, i=i: EE.tensor_scalar(
                            W2[:, i * 512:(i + 1) * 512],
                            W2[:, i * 512:(i + 1) * 512],
                            sudpsb[:, i:i + 1], FW8192_RES, Al.mult, Al.mult),
                            inc="dve")
                    P.wait("sync", "dve", vs)
                    src = W2[:, gg * 4096:(gg + 1) * 4096].rearrange(
                        "p (c d) -> p c d", d=512)
                    dstD = wd_s[dsl, gg * 8:(gg + 1) * 8].rearrange(
                        "c p d -> p c d")
                    stash_done = P.dma("sync", dstD, src, "sst0")
                w1_free_dve = v_dve_stage
                if dsl == 0 and debug_phase < 99:
                    P.wait("sync", "dve", P.v("dve"))
                    stash_done = P.dma("sync", dbg["wde"][:], W2[:], "sst0")
        p1_done = P.snap()

        if debug_phase < 1:
            finish(P, es, nc)
            return nc

        NBLK = 16 * 17 // 2
        with ExitStack() as p2:
            BUFA = p2.enter_context(nc.sbuf_tensor("BUFA", [128, EC * S_LEN], BF16))
            PALL = p2.enter_context(nc.sbuf_tensor("PALL", [128, NBLK * 128], BF16))
            WBLK = [p2.enter_context(nc.sbuf_tensor(f"WBLK{j}", [128, E], BF16))
                    for j in range(2)]
            XR = [p2.enter_context(nc.sbuf_tensor(f"XR{j}", [128, E], F32))
                  for j in range(2)]
            XNF = p2.enter_context(nc.sbuf_tensor("XNF", [128, E], BF16))
            KSTG = [p2.enter_context(nc.sbuf_tensor(f"KSTG{j}", [128, 512], BF16))
                    for j in range(4)]
            QT = [p2.enter_context(nc.sbuf_tensor(f"QT{j}", [128, EC * 128], BF16))
                  for j in range(2)]
            SS = p2.enter_context(nc.sbuf_tensor("SS", [128, S_LEN], F32))
            PB = p2.enter_context(nc.sbuf_tensor("PB", [128, S_LEN], BF16))
            OT = p2.enter_context(nc.sbuf_tensor("OT", [128, E], F32))
            X1T = p2.enter_context(nc.sbuf_tensor("X1T", [128, E], F32))
            RSA = p2.enter_context(nc.sbuf_tensor("RSA", [128, 16], F32))
            ps = [p2.enter_context(nc.psum_tensor(f"ps{j}", [128, 512], F32))
                  for j in range(6)]
            pst = [p2.enter_context(nc.psum_tensor(f"pst{j}", [128, 1024], BF16))
                   for j in range(2)]

            def bufa(c, t0, w):
                return BUFA[:, c * S_LEN + t0: c * S_LEN + t0 + w]

            def pall(qi, kb):
                base = (qi * (qi + 1) // 2 + kb) * 128
                return PALL[:, base: base + 128]

            phase_barrier()

            if debug_phase < 99 and SCRUB:
                for buf in (WBLK[0], WBLK[1], XR[0], XR[1], XNF, KSTG[0],
                            KSTG[1], KSTG[2], KSTG[3], QT[0], QT[1], SS, PB,
                            OT, X1T, BUFA, PALL):
                    P.op("vector", lambda EE, buf=buf: EE.memset(buf[:], 999.0))
                vpz = P.op("vector", lambda EE: EE.memset(RSA[:], 999.0), inc="dve")
                P.wait("sync", "dve", vpz)
                vsc = None
                for dc in range(EC):
                    for tg in range(4):
                        vsc = P.dma("sync", k_s[dc, tg], KSTG[0][:], "qs")
                for dc in range(EC):
                    vsc = P.dma("sync", v_s[dc * 128:(dc + 1) * 128, :],
                                BUFA[:, 0:S_LEN], "qs")
                for qi in range(16):
                    vsc = P.dma("sync", q_s[qi].rearrange("c p d -> p c d"),
                                QT[0].rearrange("p (c d) -> p c d", d=128), "qs")
                P.wait("sync", "qs", vsc)
                phase_barrier()

            psv_act = [0, 0]
            ps01_dve = [0, 0]
            ps01_act = [0, 0]
            kstg_sst = [0, 0, 0, 0]
            wblk_pe = [0, 0]
            ps25_dve = 0
            xr_dve = [0, 0]
            xr_act = [0, 0]
            x1st_done = 0

            for b in range(NB):
                bbase = b * S_LEN

                for tt in range(16):
                    xr = XR[tt % 2]
                    sxi = SX[tt % 2]
                    P.wait("sync", "dve", xr_dve[tt % 2])
                    P.wait("sync", "act", xr_act[tt % 2])
                    vx = P.dma("sync", xr[:],
                               x_in[bbase + tt * 128: bbase + (tt + 1) * 128, :],
                               sxi)
                    P.wait("scalar", sxi, vx)
                    v_xr, v_sd, v_rs = ln_stats(xr[:], ZERO)
                    xr_act[tt % 2] = v_xr
                    P.wait("vector", "act", v_rs)
                    vap = P.op("vector", lambda EE, xr=xr: EE.tensor_scalar(
                        XNF[:], xr[:], meanv[:], rstdv[:], Al.subtract, Al.mult),
                        inc="dve")
                    xr_dve[tt % 2] = vap
                    P.wait("tensor", "dve", vap)
                    for c in range(EC):
                        slot = c % 2
                        P.wait("tensor", "act", psv_act[slot])
                        vt = P.op("tensor", lambda EE, c=c, slot=slot: EE.transpose(
                            pst[slot][:, 0:128],
                            XNF[:, c * 128:(c + 1) * 128], ibsb[:]), inc="pe")
                        P.wait("scalar", "pe", vt)
                        ve = P.op("scalar", lambda EE, c=c, slot=slot, tt=tt: EE.copy(
                            bufa(c, tt * 128, 128),
                            pst[slot][:, 0:128]), inc="act")
                        psv_act[slot] = ve
                    P.wait("vector", "pe", P.v("pe"))
                a1_act = P.v("act")

                if debug_phase < 2:
                    P.wait("sync", "act", a1_act)
                    P.dma("sync", dbg["xnT"][:], BUFA[:], "qs")
                    finish(P, es, nc)
                    return nc

                def proj_phase(w_sw, store_fn):
                    for dc in range(EC):
                        wb = WBLK[dc % 2]
                        qpi = QP[dc % 2]
                        P.wait("gpsimd", "pe", wblk_pe[dc % 2])
                        vw = P.dma("gpsimd", wb[:], w_sw[dc], qpi)
                        for tg in range(4):
                            gx = dc * 4 + tg
                            pp = ps[gx % 2]
                            P.wait("tensor", qpi, vw)
                            if gx == 0:
                                P.wait("tensor", "act", a1_act)
                            P.wait("tensor", "act", ps01_act[gx % 2])
                            P.wait("tensor", "dve", ps01_dve[gx % 2])
                            for c in range(EC):
                                vm = P.op("tensor", lambda EE, wb=wb, c=c, tg=tg, pp=pp: EE.matmul(
                                    pp[:], wb[:, c * 128:(c + 1) * 128],
                                    bufa(c, tg * 512, 512),
                                    start=(c == 0), stop=(c == EC - 1)),
                                    inc="pe" if c == EC - 1 else None)
                            wblk_pe[dc % 2] = vm
                            sslot = gx % 4
                            P.wait("scalar", "pe", vm)
                            P.wait("scalar", SST[sslot], kstg_sst[sslot])
                            ve = P.op("scalar", lambda EE, sslot=sslot, pp=pp: EE.copy(
                                KSTG[sslot][:], pp[:]), inc="act")
                            ps01_act[gx % 2] = ve
                            P.wait("sync", "act", ve)
                            kstg_sst[sslot] = store_fn(dc, tg, sslot)

                def store_v(dc, tg, sslot):
                    return P.dma("sync", v_s[dc * 128:(dc + 1) * 128,
                                             tg * 512:(tg + 1) * 512],
                                 KSTG[sslot][:], SST[sslot])

                def store_k(dc, tg, sslot):
                    return P.dma("sync", k_s[dc, tg], KSTG[sslot][:], SST[sslot])

                def store_q(dc, tg, sslot):
                    if P1_TRUNC == "qsplit":
                        vq = None
                        for qsub in range(4):
                            vq = P.dma("sync", q_s[tg * 4 + qsub, dc],
                                       KSTG[sslot][:, qsub * 128:(qsub + 1) * 128],
                                       SST[sslot])
                        return vq
                    dstq = q_s[tg * 4:(tg + 1) * 4, dc].rearrange("q p d -> p q d")
                    srcq = KSTG[sslot].rearrange("p (q d) -> p q d", d=128)
                    return P.dma("sync", dstq, srcq[:, :, :], SST[sslot])

                proj_phase(wv_sw, store_v)
                proj_phase(wq_sw, store_q)
                proj_phase(wk_sw, store_k)



                P.wait("sync", "pe", P.v("pe"))
                for j in range(4):
                    P.wait("sync", SST[j], P.v(SST[j]))


                kvv = k_s.rearrange("c g p t -> p c g t")
                bview = BUFA.rearrange("p (c g t) -> p c g t", g=4, t=512)
                kt_loaded = P.dma("sync", bview, kvv, "srl")

                if debug_phase < 3:
                    P.wait("sync", "srl", kt_loaded)
                    P.dma("sync", dbg["kT"][:], BUFA[:], "qs")
                    finish(P, es, nc)
                    return nc

                qt_marks = {}
                smm_mark = {}
                pb_pe = 0
                vex_prev = 0
                rsum_free = 0
                for qi in range(16):
                    L = (qi + 1) * 128
                    G = (L + 511) // 512
                    sqi = SQ[qi % 2]
                    if qi >= 2:
                        P.wait("sync", "pe", smm_mark[qi - 2])
                    qin = q_s[qi].rearrange("c p d -> p c d")
                    qtd = QT[qi % 2].rearrange("p (c d) -> p c d", d=128)
                    qt_marks[qi] = P.dma("sync", qtd, qin, sqi)
                    qt = QT[qi % 2]
                    P.wait("tensor", sqi, qt_marks[qi])
                    P.wait("tensor", "srl", kt_loaded)
                    for g in range(G):
                        w = min(512, L - g * 512)
                        pp = ps[g % 2]
                        P.wait("tensor", "dve", ps01_dve[g % 2])
                        P.wait("tensor", "act", ps01_act[g % 2])
                        for c in range(EC):
                            vm = P.op("tensor", lambda EE, qt=qt, c=c, g=g, w=w, pp=pp: EE.matmul(
                                pp[:, 0:w], qt[:, c * 128:(c + 1) * 128],
                                bufa(c, g * 512, w),
                                start=(c == 0), stop=(c == EC - 1)),
                                inc="pe" if c == EC - 1 else None)
                        smm_mark[qi] = vm
                        P.wait("vector", "pe", vm)
                        if g == 0:
                            P.wait("vector", "act", vex_prev)
                        vse = P.op("vector", lambda EE, g=g, w=w, pp=pp: EE.tensor_copy(
                            SS[:, g * 512: g * 512 + w], pp[:, 0:w]), inc="dve")
                        ps01_dve[g % 2] = vse
                    P.op("vector", lambda EE, qi=qi: EE.tensor_tensor(
                        SS[:, qi * 128:(qi + 1) * 128],
                        SS[:, qi * 128:(qi + 1) * 128], masksb[:], Al.add))
                    P.op("vector", lambda EE, L=L: EE.reduce_max(
                        nmx[:], SS[:, 0:L], axis=AX.X, negate=True))
                    vnm = P.op("vector", lambda EE: EE.tensor_scalar(
                        nmxs[:], nmx[:], C_SM, None, Al.mult), inc="dve")
                    P.wait("scalar", "dve", vnm)
                    P.wait("scalar", "pe", pb_pe)
                    vex = P.op("scalar", lambda EE, L=L: EE.activation(
                        PB[:, 0:L], SS[:, 0:L], AF.Exp,
                        bias=nmxs[:], scale=C_SM, accum_out=sums[:]), inc="act")
                    vex_prev = vex
                    P.wait("scalar", "dve", rsum_free)
                    vrs = P.op("scalar", lambda EE: EE.copy(
                        rsum[:], sums[:]), inc="act")
                    P.wait("vector", "act", vrs)
                    rsum_free = P.op("vector", lambda EE, qi=qi: EE.reciprocal(
                        RSA[:, qi:qi + 1], rsum[:]), inc="dve")
                    P.wait("tensor", "act", vex)
                    for kb in range(qi + 1):
                        slot = kb % 2
                        P.wait("tensor", "act", psv_act[slot])
                        vt = P.op("tensor", lambda EE, kb=kb, slot=slot: EE.transpose(
                            pst[slot][:, 0:128],
                            PB[:, kb * 128:(kb + 1) * 128], ibsb[:]), inc="pe")
                        P.wait("scalar", "pe", vt)
                        ve = P.op("scalar", lambda EE, qi=qi, kb=kb, slot=slot: EE.copy(
                            pall(qi, kb),
                            pst[slot][:, 0:128]), inc="act")
                        psv_act[slot] = ve
                    pb_pe = P.v("pe")
                pall_act = P.v("act")

                P.wait("sync", "pe", P.v("pe"))
                va = None
                for kb in range(16):
                    if va:
                        P.wait("sync", "srl", va)
                    va = P.dmat("sync", BUFA[:, kb * E:(kb + 1) * E],
                                v_s[:, kb * 128:(kb + 1) * 128], "srl")
                v_loaded = va

                if debug_phase < 4:
                    P.wait("sync", "srl", v_loaded)
                    P.dma("sync", dbg["v"][:], BUFA[:], "qs")
                    finish(P, es, nc)
                    return nc

                def vblk(kb, d0, w):
                    return BUFA[:, kb * E + d0: kb * E + d0 + w]

                otfree_act = 0
                for idx, qi in enumerate(range(15, -1, -1)):
                    nkb = qi + 1
                    sxi = SX[qi % 2]
                    P.wait("sync", "dve", xr_dve[qi % 2])
                    P.wait("sync", "act", xr_act[qi % 2])
                    vxr = P.dma("sync", XR[qi % 2][:],
                                x_in[bbase + qi * 128: bbase + (qi + 1) * 128, :],
                                sxi)
                    xr_act[qi % 2] = 0
                    P.wait("tensor", "act", pall_act)
                    P.wait("tensor", "srl", v_loaded)
                    P.wait("tensor", "dve", ps25_dve)
                    for kb in range(nkb):
                        for dg in range(4):
                            vm = P.op("tensor", lambda EE, qi=qi, kb=kb, dg=dg, nkb=nkb: EE.matmul(
                                ps[2 + dg][:], pall(qi, kb),
                                vblk(kb, dg * 512, 512),
                                start=(kb == 0), stop=(kb == nkb - 1)),
                                inc="pe" if (kb == nkb - 1 and dg == 3) else None)
                    P.wait("vector", "pe", vm)
                    P.wait("vector", "act", otfree_act)
                    oev = None
                    for dg in range(4):
                        oev = P.op("vector", lambda EE, dg=dg, qi=qi: EE.tensor_scalar(
                            OT[:, dg * 512:(dg + 1) * 512], ps[2 + dg][:],
                            RSA[:, qi:qi + 1], None, Al.mult), inc="dve")
                    ps25_dve = oev
                    v_xr, v_sd, v_rs = ln_stats(OT[:], ZERO)
                    otfree_act = v_xr
                    P.wait("vector", "act", v_rs)
                    P.wait("vector", sxi, vxr)
                    P.op("vector", lambda EE: EE.tensor_scalar(
                        OT[:], OT[:], meanv[:], rstdv[:], Al.subtract, Al.mult))
                    P.wait("vector", "sxs", x1st_done)
                    vx1 = P.op("vector", lambda EE, qi=qi: EE.tensor_tensor(
                        X1T[:], OT[:], XR[qi % 2][:], Al.add), inc="dve")
                    xr_dve[qi % 2] = vx1
                    P.wait("gpsimd", "dve", vx1)
                    x1st_done = P.dma(
                        "gpsimd", x1_s[bbase + qi * 128: bbase + (qi + 1) * 128, :],
                        X1T[:], "sxs")

                if debug_phase < 5 and b == 0:
                    P.wait("sync", "sxs", P.v("sxs"))
                    P.dma("sync", dbg["x1"][:], x1_s[0:S_LEN, :], "qs")
                    finish(P, es, nc)
                    return nc

                bs = P.snap()
                P.wait_all("sync", {"pe": bs["pe"], "dve": bs["dve"], "act": bs["act"]})
                P.wait_all("gpsimd", {"pe": bs["pe"], "dve": bs["dve"], "act": bs["act"]})

        p2_done = P.snap()
        if debug_phase < 5:
            finish(P, es, nc)
            return nc

        with ExitStack() as p3:
            XN2T = p3.enter_context(nc.sbuf_tensor("XN2T", [128, EC * T_CORE], BF16))
            with ExitStack() as p3a:
                AT = p3a.enter_context(nc.sbuf_tensor("AT", [128, E], F32))
                BT = p3a.enter_context(nc.sbuf_tensor("BT", [128, E], F32))
                X1R = [p3a.enter_context(nc.sbuf_tensor(f"X1R{j}", [128, E], F32))
                       for j in range(2)]
                TS1 = p3a.enter_context(nc.sbuf_tensor("TS1", [128, EC * 512], BF16))
                psF = [p3a.enter_context(nc.psum_tensor(f"psF{j}", [128, 512], F32))
                       for j in range(2)]
                psH = [p3a.enter_context(nc.psum_tensor(f"psH{j}", [128, 512], F32))
                       for j in range(2)]

                phase_barrier()
                psf_act = [0, 0]
                hev = [0, 0]
                x1r_dve = [0, 0]
                x1r_act = [0, 0]
                bt_pe = 0
                for tg in range(8):
                    ts1_last = 0
                    for tsub in range(4):
                        ti = tg * 4 + tsub
                        sxi = SX[ti % 2]
                        P.wait("sync", "dve", x1r_dve[ti % 2])
                        P.wait("sync", "act", x1r_act[ti % 2])
                        vx = P.dma("sync", X1R[ti % 2][:],
                                   x1_s[ti * 128:(ti + 1) * 128, :], sxi)
                        P.wait("scalar", sxi, vx)
                        v_xr, v_sd, v_rs = ln_stats(X1R[ti % 2][:], LNC2)
                        x1r_act[ti % 2] = v_xr
                        P.wait("vector", "act", v_rs)
                        P.wait("vector", "pe", bt_pe)
                        vap = P.op("vector", lambda EE, ti=ti: EE.tensor_scalar(
                            AT[:], X1R[ti % 2][:], meanv[:], rstdv[:],
                            Al.subtract, Al.mult), inc="dve")
                        x1r_dve[ti % 2] = vap
                        P.op("vector", lambda EE: EE.tensor_tensor(
                            BT[:], AT[:], suugb[:], Al.mult))
                        bufs = [BT, AT]
                        vbf = None
                        for si, h in enumerate([1, 2, 4, 8]):
                            src = bufs[si % 2]
                            dst = bufs[(si + 1) % 2]
                            sv = src.rearrange("p (blk two hf) -> p blk two hf",
                                               two=2, hf=h * 128)
                            dv = dst.rearrange("p (blk two hf) -> p blk two hf",
                                               two=2, hf=h * 128)
                            P.op("vector", lambda EE, sv=sv, dv=dv: EE.tensor_tensor(
                                dv[:, :, 0, :], sv[:, :, 0, :], sv[:, :, 1, :],
                                Al.add))
                            vbf = P.op("vector", lambda EE, sv=sv, dv=dv: EE.tensor_tensor(
                                dv[:, :, 1, :], sv[:, :, 0, :], sv[:, :, 1, :],
                                Al.subtract), inc="dve")
                        P.wait("tensor", "dve", vbf)
                        for c in range(EC):
                            slot = c % 2
                            P.wait("tensor", "act", psf_act[slot])
                            vt = P.op("tensor", lambda EE, c=c, slot=slot: EE.transpose(
                                psF[slot][:, 0:128],
                                BT[:, c * 128:(c + 1) * 128], i32sb[:]), inc="pe")
                            P.wait("scalar", "pe", vt)
                            ve = P.op("scalar", lambda EE, c=c, tsub=tsub, slot=slot: EE.copy(
                                TS1[:, c * 512 + tsub * 128: c * 512 + (tsub + 1) * 128],
                                psF[slot][:, 0:128]), inc="act")
                            psf_act[slot] = ve
                            ts1_last = ve
                        bt_pe = P.v("pe")
                    P.wait("tensor", "act", ts1_last)
                    for c in range(EC):
                        pp = psH[c % 2]
                        P.wait("tensor", "act", hev[c % 2])
                        vm = P.op("tensor", lambda EE, c=c, pp=pp: EE.matmul(
                            pp[:], h128sb[:], TS1[:, c * 512:(c + 1) * 512],
                            start=True, stop=True), inc="pe")
                        P.wait("scalar", "pe", vm)
                        ve = P.op("scalar", lambda EE, c=c, tg=tg, pp=pp: EE.copy(
                            XN2T[:, c * T_CORE + tg * 512: c * T_CORE + (tg + 1) * 512],
                            pp[:]), inc="act")
                        hev[c % 2] = ve
                xn2_act = P.v("act")

                if debug_phase < 6:
                    P.wait("sync", "act", xn2_act)
                    P.dma("sync", dbg["xn2T"][:], XN2T[:], "qs")
                    finish(P, es, nc)
                    return nc

            with ExitStack() as p4:
                UB = [p4.enter_context(nc.sbuf_tensor(f"UB{j}", [128, E], BF16))
                      for j in range(2)]
                GB = [p4.enter_context(nc.sbuf_tensor(f"GB{j}", [128, E], BF16))
                      for j in range(2)]
                USTG = p4.enter_context(nc.sbuf_tensor("USTG", [128, 1024], F32))
                GSTG = p4.enter_context(nc.sbuf_tensor("GSTG", [128, 1024], F32))
                ZC = p4.enter_context(nc.sbuf_tensor("ZC", [128, 1024], BF16))
                psu = [p4.enter_context(nc.psum_tensor(f"psu{j}", [128, 512], F32))
                       for j in range(2)]
                psg = [p4.enter_context(nc.psum_tensor(f"psg{j}", [128, 512], F32))
                       for j in range(2)]

                phase_barrier()
                ug_pe = [0, 0]
                zev = [0, 0]
                gev = [0, 0]
                uev = [0, 0]
                zst = [0, 0]
                for ic in range(ICN):
                    qpi = QP[ic % 2]
                    P.wait("gpsimd", "pe", ug_pe[ic % 2])
                    P.dma("gpsimd", UB[ic % 2][:], wup_sw[ic], qpi)
                    vlg = P.dma("gpsimd", GB[ic % 2][:], wgate_sw[ic], qpi)
                    P.wait("tensor", qpi, vlg)
                    for tgl in range(8):
                        sl = tgl % 2
                        ppu = psu[sl]
                        ppg = psg[sl]
                        P.wait("tensor", "dve", uev[sl])
                        P.wait("tensor", "act", gev[sl])
                        for c in range(EC):
                            P.op("tensor", lambda EE, ic=ic, c=c, tgl=tgl, ppu=ppu: EE.matmul(
                                ppu[:], UB[ic % 2][:, c * 128:(c + 1) * 128],
                                XN2T[:, c * T_CORE + tgl * 512: c * T_CORE + (tgl + 1) * 512],
                                start=(c == 0), stop=(c == EC - 1)))
                        for c in range(EC):
                            vm = P.op("tensor", lambda EE, ic=ic, c=c, tgl=tgl, ppg=ppg: EE.matmul(
                                ppg[:], GB[ic % 2][:, c * 128:(c + 1) * 128],
                                XN2T[:, c * T_CORE + tgl * 512: c * T_CORE + (tgl + 1) * 512],
                                start=(c == 0), stop=(c == EC - 1)),
                                inc="pe" if c == EC - 1 else None)
                        if tgl == 7:
                            ug_pe[ic % 2] = vm
                        P.wait("vector", "pe", vm)
                        P.wait("vector", SO[sl], zst[sl])
                        vu = P.op("vector", lambda EE, ic=ic, sl=sl, ppu=ppu: EE.tensor_scalar(
                            USTG[:, sl * 512:(sl + 1) * 512], ppu[:],
                            wupsc[:, ic:ic + 1], None, Al.mult), inc="dve")
                        uev[sl] = vu
                        P.wait("scalar", "pe", vm)
                        P.wait("scalar", "dve", zev[sl])
                        vg = P.op("scalar", lambda EE, ic=ic, sl=sl, ppg=ppg: EE.activation(
                            GSTG[:, sl * 512:(sl + 1) * 512], ppg[:], AF.Silu,
                            bias=0.0, scale=wgatesc[:, ic:ic + 1]), inc="act")
                        gev[sl] = vg
                        P.wait("vector", "act", vg)
                        vz = P.op("vector", lambda EE, sl=sl: EE.tensor_tensor(
                            ZC[:, sl * 512:(sl + 1) * 512],
                            USTG[:, sl * 512:(sl + 1) * 512],
                            GSTG[:, sl * 512:(sl + 1) * 512], Al.mult), inc="dve")
                        zev[sl] = vz
                        P.wait("scalar", "dve", vz)
                        zst[sl] = P.op("scalar", lambda EE, ic=ic, tgl=tgl, sl=sl: EE.dma_start(
                            out=z_s[ic][:, tgl * 512:(tgl + 1) * 512],
                            in_=ZC[:, sl * 512:(sl + 1) * 512]), inc=SO[sl], n=16)
                if debug_phase < 7:
                    P.wait("sync", "so0", P.v("so0"))
                    P.wait("sync", "so1", P.v("so1"))
                    P.dma("sync", dbg["z"][:], z_s[0], "qs")
                    finish(P, es, nc)
                    return nc
        p4_done = P.snap()

        with ExitStack() as p5:
            ZT = p5.enter_context(nc.sbuf_tensor("ZT", [128, ICN * 512], BF16))
            WDG = [p5.enter_context(nc.sbuf_tensor(f"WDG{j}", [128, 8 * 512], BF16))
                   for j in range(2)]
            OTD = p5.enter_context(nc.sbuf_tensor("OTD", [128, 4 * E], F32))
            X1R2 = [p5.enter_context(nc.sbuf_tensor(f"X1R2{j}", [128, E], F32))
                    for j in range(2)]
            OUTT = [p5.enter_context(nc.sbuf_tensor(f"OUTT{j}", [128, E], F32))
                    for j in range(2)]
            psB = [p5.enter_context(nc.psum_tensor(f"psB{j}", [128, 2048], F32))
                   for j in range(2)]

            phase_barrier()
            otv = OTD.rearrange("p (ts d) -> p ts d", d=E)
            wdg_pe = [0, 0]
            zg_marks = {}
            wg_marks = {}
            oevac = [0, 0]
            ot_act = 0
            tile_done = 0
            x1r2_dve = [0, 0]
            outt_so = [0, 0]
            zgi = 0
            wgi = 0
            for tile in range(8):
                P.wait("gpsimd", "pe", tile_done)
                ztile_groups = []
                for gg in range(8):
                    qpi = QP[zgi % 2]
                    if zgi >= 2:
                        P.wait("gpsimd", qpi, zg_marks[zgi - 2])
                    src = z_s[gg * 8:(gg + 1) * 8, :,
                              tile * 512:(tile + 1) * 512].rearrange(
                                  "c p t -> p c t")
                    dstz = ZT[:, gg * 4096:(gg + 1) * 4096].rearrange(
                        "p (c t) -> p c t", t=512)
                    zg_marks[zgi] = P.dma("gpsimd", dstz, src, qpi)
                    ztile_groups.append((QP[zgi % 2], zg_marks[zgi]))
                    zgi += 1
                for dp in range(4):
                    ppB = psB[dp % 2]
                    P.wait("tensor", "dve", oevac[dp % 2])
                    for gg in range(8):
                        sqi = SQ[wgi % 2]
                        P.wait("sync", "pe", wdg_pe[wgi % 2])
                        srcw = wd_s[dp, gg * 8:(gg + 1) * 8].rearrange(
                            "c p d -> p c d")
                        dstw = WDG[wgi % 2].rearrange("p (c d) -> p c d", d=512)
                        vw = P.dma("sync", dstw, srcw, sqi)
                        wg_marks[wgi] = vw
                        P.wait("tensor", sqi, vw)
                        if dp == 0:
                            zs, zm = ztile_groups[gg]
                            P.wait("tensor", zs, zm)
                        wb = WDG[wgi % 2]
                        for cc in range(8):
                            ic = gg * 8 + cc
                            for tsub in range(4):
                                vm = P.op("tensor", lambda EE, ic=ic, cc=cc, tsub=tsub, ppB=ppB, wb=wb: EE.matmul(
                                    ppB[:, tsub * 512:(tsub + 1) * 512],
                                    ZT[:, ic * 512 + tsub * 128: ic * 512 + (tsub + 1) * 128],
                                    wb[:, cc * 512:(cc + 1) * 512],
                                    start=(ic == 0), stop=(ic == ICN - 1)),
                                    inc="pe" if tsub == 3 else None)
                        wdg_pe[wgi % 2] = vm
                        wgi += 1
                    tile_done = vm
                    P.wait("vector", "pe", vm)
                    P.wait("vector", "act", ot_act)
                    vo = P.op("vector", lambda EE, dp=dp, ppB=ppB, otv=otv: EE.tensor_copy(
                        otv[:, :, dp * 512:(dp + 1) * 512],
                        ppB.rearrange("p (ts d) -> p ts d", d=512)[:, :, :]),
                        inc="dve")
                    oevac[dp % 2] = vo
                for tsub in range(4):
                    ti = tile * 4 + tsub
                    sxi = SX[tsub % 2]
                    P.wait("sync", "dve", x1r2_dve[tsub % 2])
                    vx = P.dma("sync", X1R2[tsub % 2][:],
                               x1_s[ti * 128:(ti + 1) * 128, :], sxi)
                    v_xr, v_sd, v_rs = ln_stats(otv[:, tsub, :], ZERO)
                    ot_act = v_xr
                    P.wait("vector", "act", v_rs)
                    P.wait("vector", sxi, vx)
                    P.op("vector", lambda EE, tsub=tsub, otv=otv: EE.tensor_scalar(
                        otv[:, tsub, :], otv[:, tsub, :], meanv[:], rstdv[:],
                        Al.subtract, Al.mult))
                    P.wait("vector", SO[tsub % 2], outt_so[tsub % 2])
                    vf = P.op("vector", lambda EE, tsub=tsub, otv=otv: EE.tensor_tensor(
                        OUTT[tsub % 2][:], otv[:, tsub, :], X1R2[tsub % 2][:],
                        Al.add), inc="dve")
                    x1r2_dve[tsub % 2] = vf
                    P.wait("gpsimd", "dve", vf)
                    outt_so[tsub % 2] = P.dma(
                        "gpsimd", out[ti * 128:(ti + 1) * 128, :],
                        OUTT[tsub % 2][:], SO[tsub % 2])

        finish(P, es, nc)
    return nc


DMA_SEMS = ("qs", "sx0", "sx1", "sq0", "sq1", "sst0", "sst1", "sst2", "sst3",
            "srl", "so0", "so1", "sxs", "qp", "qp2")


def finish(P, es, nc):
    fin = P.snap()
    P.wait_all("sync", {k: v for k, v in fin.items() if k in DMA_SEMS})
    nums = sorted(h.num for h in P.semh.values())
    assert nums == list(range(nums[0], nums[0] + len(nums))), nums
    rng = range(nums[0], nums[-1] + 1)

    def _boot(g, rng=rng):
        g.dma_reset(rng)
        g.sem_clear(rng)

    with nc.Block() as boot:
        boot.gpsimd(_boot)
    blk = es.enter_context(nc.Block())
    P.replay(blk)


_NC_CACHE = {}


def _get_nc(debug_phase=99):
    if debug_phase not in _NC_CACHE:
        _NC_CACHE[debug_phase] = build_program(debug_phase)
    return _NC_CACHE[debug_phase]


def _hadamard(n):
    h = np.array([[1.0]], dtype=np.float64)
    while h.shape[0] < n:
        h = np.block([[h, h], [h, -h]])
    return h


def prep_shared(inputs):
    f32 = np.float32
    d = {}

    def colblock(wt, nblk):
        e = wt.shape[0]
        ec = e // 128
        outp = np.empty((nblk, 128, e), dtype=f32)
        for j in range(nblk):
            blk = wt[:, j * 128:(j + 1) * 128]
            outp[j] = blk.reshape(ec, 128, 128).transpose(1, 0, 2).reshape(128, e)
        return outp

    d["wq_sw"] = colblock(np.ascontiguousarray(np.asarray(inputs["L0_attn_Wq"], f32).T), EC)
    d["wk_sw"] = colblock(np.ascontiguousarray(np.asarray(inputs["L0_attn_Wk"], f32).T), EC)
    d["wv_sw"] = colblock(np.ascontiguousarray(np.asarray(inputs["L0_attn_Wv"], f32).T), EC)
    d["wup_sw"] = colblock(np.ascontiguousarray(np.asarray(inputs["L0_ffn_W_up"], f32).T), ICN)
    d["wgate_sw"] = colblock(np.ascontiguousarray(np.asarray(inputs["L0_ffn_W_gate"], f32).T), ICN)
    d["wdown_t"] = np.ascontiguousarray(np.asarray(inputs["L0_ffn_W_down"], f32).T)
    su_ug = np.asarray(inputs["L0_ffn_SU_ug"], f32)
    su_dp = np.asarray(inputs["L0_ffn_SU_dp"], f32)
    wsc = np.asarray(inputs["L0_ffn_Wscale_ug"], f32)
    wsc_dp = np.asarray(inputs["L0_ffn_Wscale_dp"], f32)
    d["su_ug_b"] = np.ascontiguousarray(np.broadcast_to(su_ug, (128, E)))
    d["su_dp_pc"] = np.ascontiguousarray(su_dp.reshape(ICN, 128).T)
    d["wsc_up_pc"] = np.ascontiguousarray(wsc[:I_DIM].reshape(ICN, 128).T)
    d["wsc_gate_pc"] = np.ascontiguousarray(wsc[I_DIM:].reshape(ICN, 128).T)
    d["wsc_dp_b"] = np.ascontiguousarray(np.broadcast_to(wsc_dp, (128, E)))
    d["ident32"] = np.eye(128, dtype=f32)
    d["identb"] = np.eye(128, dtype=ml_dtypes.bfloat16)
    d["h128b"] = (_hadamard(128) * 0.125).astype(ml_dtypes.bfloat16)
    d["maskd"] = (np.triu(np.ones((128, 128)), 1) * NEG_INF).astype(f32)
    cc = np.zeros((128, 4), dtype=f32)
    cc[:, 0] = LN_EPS
    cc[:, 1] = np.log(FW2048_RES)
    cc[:, 2] = 0.0
    cc[:, 3] = 1.0 / 2048.0
    d["cconst"] = cc
    return d


def kernel(**inputs):
    nc = _get_nc()
    shared = prep_shared(inputs)
    x = np.asarray(inputs["L0_attn_x"], np.float32)
    in_maps = []
    for c in range(8):
        m = dict(shared)
        m["x_in"] = np.ascontiguousarray(x[2 * c:2 * c + 2].reshape(T_CORE, E))
        in_maps.append(m)
    res = run_bass_kernel_spmd(nc, in_maps, core_ids=list(range(8)))
    outs = [res.results[c]["out"].reshape(2, S_LEN, E) for c in range(8)]
    return np.concatenate(outs, axis=0)
